# version 03
# baseline (speedup 1.0000x reference)
import numpy as np
import ml_dtypes
import concourse.bass as bass
import concourse.mybir as mybir
from concourse.bass_utils import run_bass_kernel_spmd
from contextlib import ExitStack

F32 = mybir.dt.float32
BF16 = mybir.dt.bfloat16
AF = mybir.ActivationFunctionType
ALU = mybir.AluOpType
BFD = ml_dtypes.bfloat16

B, T, D, NH, HD, HID = 2, 2048, 2048, 32, 64, 8192
NTT, NCH, QC = 16, 4, 512
SHIFT = 8.0
SCL = 0.125
GSC = float(1.0 / np.sqrt(HID))

ENGS = ("sync", "tensor", "vector", "scalar", "gpsimd")


class Sem:
    __slots__ = ("h", "n", "key")

    def __init__(self, h, key):
        self.h, self.n, self.key = h, 0, key

    def bump(self, k):
        self.n += k
        return self.n


class Plan:
    def __init__(self):
        self.ops = {e: [] for e in ENGS}
        self.hwm = {}

    def emit(self, eng, fn):
        self.ops[eng].append(fn)

    def wait(self, eng, tok):
        if tok is None:
            return
        sem, val = tok
        if val <= 0:
            return
        k = (eng, sem.key)
        if self.hwm.get(k, -1) >= val:
            return
        self.hwm[k] = val
        self.emit(eng, lambda e, s=sem.h, v=val: e.wait_ge(s, v))

    def run(self, blk):
        for eng in ENGS:
            ops_ = self.ops[eng]

            def mk(o):
                def body(e):
                    for f in o:
                        f(e)
                return body
            getattr(blk, eng)(mk(ops_))


class Bank:
    def __init__(self, ap):
        self.ap = ap
        self.wr = None
        self.rds = []

    def gw(self, p, eng):
        for t in self.rds:
            p.wait(eng, t)
        self.rds = []

    def gr(self, p, eng):
        p.wait(eng, self.wr)


def build_nc():
    nc = bass.Bass("TRN2", target_bir_lowering=False)
    p = Plan()
    es = ExitStack()

    def din(name, shape, dt=F32):
        return nc.dram_tensor(name, shape, dt, kind="ExternalInput").ap()

    x_in = din("x", [B, T, D])
    w_in = {n: din(n, [D, D]) for n in ("wq", "wk", "wv", "wo")}
    bo_in = din("bo", [1, D])
    w_in["w1"] = din("w1", [HID, D])
    w_in["w2"] = din("w2", [D, HID])
    w_in["w3"] = din("w3", [HID, D])
    cost_in = din("cost_t", [128, T])
    sint_in = din("sint_t", [128, T])
    tri_in = din("tri_t", [128, 128], BF16)
    hmat_in = din("hmat_t", [128, 128], BF16)
    out_t = nc.dram_tensor("out", [B, T, D], F32, kind="ExternalOutput").ap()

    def scr(name, shape, dt=BF16):
        return nc.dram_tensor(name, shape, dt, kind="Internal").ap()

    w_bf = {n: scr(n + "_bf", list(w_in[n].shape))
            for n in ("wq", "wk", "wv", "wo", "w1", "w2", "w3")}
    img = {"wq": scr("wq_img", [16, 4, 128, 512]),
           "wk": scr("wk_img", [16, 4, 128, 512]),
           "wv": scr("wv_img", [16, 4, 128, 512]),
           "wo": scr("wo_img", [16, 4, 128, 512]),
           "w1": scr("w1_img", [16, 16, 128, 512]),
           "w3": scr("w3_img", [16, 16, 128, 512]),
           "w2": scr("w2_img", [64, 4, 128, 512])}
    w2mix = scr("w2mix", [64, D, 128])
    xn1_s = scr("xn1_s", [B, T, D])
    xn2_s = scr("xn2_s", [B, T, D])
    kT_s = scr("kT_s", [B, D, T])
    va_s = scr("va_s", [B, T, D])
    x1_s = scr("x1_s", [B, T, D], F32)

    sems = {}
    for n in ("dqs", "dqa", "dqg", "spe", "ssc", "sdv", "sgp"):
        sems[n] = Sem(es.enter_context(nc.semaphore(n)), n)
    dqs, dqa, dqg = sems["dqs"], sems["dqa"], sems["dqg"]
    spe, ssc, sdv = sems["spe"], sems["ssc"], sems["sdv"]
    QENG = {"dqs": "sync", "dqa": "scalar", "dqg": "gpsimd"}
    ESEM = {"tensor": spe, "scalar": ssc, "vector": sdv, "gpsimd": sems["sgp"]}

    def dma(q, out_ap, in_ap):
        v = q.bump(16)
        p.emit(QENG[q.key], lambda e, o=out_ap, i=in_ap, s=q.h:
               e.dma_start(o, i).then_inc(s, 16))
        return (q, v)

    def dmat(out_ap, in_ap):
        v = dqs.bump(16)
        p.emit("sync", lambda e, o=out_ap, i=in_ap, s=dqs.h:
               e.dma_start_transpose(o, i).then_inc(s, 16))
        return (dqs, v)

    def op(eng, fn, inc=False):
        if inc:
            s = ESEM[eng]
            v = s.bump(1)
            p.emit(eng, lambda e, f=fn, sh=s.h: f(e).then_inc(sh, 1))
            return (s, v)
        p.emit(eng, fn)
        return None

    def wt(eng, tok):
        p.wait(eng, tok)

    def barrier():
        toks = [(s, s.n) for s in sems.values()]
        for eng in ENGS:
            for t in toks:
                p.wait(eng, t)

    def sb(name, shape, dt=F32, scope=None):
        return (scope or es).enter_context(nc.sbuf_tensor(name, shape, dt))

    cpool = sb("cpool", [128, 4])
    hmat = sb("hmat", [128, 128], BF16)
    onesb = sb("onesb", [1, 128], BF16)
    xnt = [sb(f"xnt{i}", [128, 16 * 512], BF16) for i in range(2)]
    era = ExitStack()
    costb = sb("costb", [128, T], BF16, scope=era)
    sintb = sb("sintb", [128, T], BF16, scope=era)
    tri4 = sb("tri4", [128, 512], BF16, scope=era)
    BO = sb("BO", [128, D], scope=era)
    statb = sb("statb", [128, 64], scope=era)
    xb = [sb(f"xb{i}", [128, D], scope=era) for i in range(2)]
    xnst = [sb(f"xnst{i}", [128, D], BF16, scope=era) for i in range(2)]
    slab = [sb(f"slab{i}", [128, 16 * 512], BF16, scope=era) for i in range(2)]
    prow = [sb(f"prow{i}", [128, 512], BF16, scope=era) for i in range(2)]
    vstg = [sb(f"vstg{i}", [128, 512], BF16, scope=era) for i in range(2)]
    ropec = sb("ropec", [128, 512], BF16, scope=era)
    tbuf = [sb(f"tbuf{i}", [128, 512], BF16, scope=era) for i in range(4)]

    pb = [Bank(es.enter_context(nc.psum_tensor(f"pb{i}", [128, 512], F32)))
          for i in range(8)]

    U = {"xnt": [None, None], "slab": [None, None], "xb": [None, None],
         "xnst": [None, None], "prow": [None, None], "tbuf": [None] * 4,
         "stat": [None, None], "vstg": [None, None]}

    op("vector", lambda e: e.memset(cpool[:, 0:1], -SHIFT))
    op("vector", lambda e: e.memset(cpool[:, 1:2], 1e-5))
    t_ones = op("vector", lambda e: e.memset(onesb[0:1, :], 1.0), inc=True)
    nc.const_aps.register(F32, -SHIFT, cpool[:, 0:1])
    nc.const_aps.register(F32, 1e-5, cpool[:, 1:2])

    init_scope = ExitStack()
    tblf = sb("tblf", [128, T], scope=init_scope)
    bo_st = sb("bo_st", [1, D], scope=init_scope)
    bo_bf = sb("bo_bf", [1, D], BF16, scope=init_scope)
    wt2r = sb("wt2r", [64, 32 * 128], BF16, scope=init_scope)
    fstg = [sb(f"fstg{i}", [64, 512], BF16, scope=init_scope) for i in range(2)]

    tk = dma(dqa, tblf[:], cost_in)
    wt("vector", tk)
    t_cos = op("vector", lambda e: e.tensor_copy(costb[:], tblf[:]), inc=True)
    tk = dma(dqa, tblf[:], sint_in)
    wt("vector", t_cos)
    wt("vector", tk)
    t_sin = op("vector", lambda e: e.tensor_copy(sintb[:], tblf[:]), inc=True)
    tk = dma(dqa, tri4[:, 0:128], tri_in)
    wt("vector", tk)
    t_tri = None
    for r in range(1, 4):
        wt("vector", t_tri)
        t_tri = op("vector", lambda e, r=r: e.tensor_copy(
            tri4[:, r * 128:(r + 1) * 128], tri4[:, 0:128]), inc=True)
    t_hm = dma(dqa, hmat[:], hmat_in)
    tk = dma(dqa, bo_st[:], bo_in)
    wt("vector", tk)
    t_bobf = op("vector", lambda e: e.tensor_copy(bo_bf[:], bo_st[:]), inc=True)
    wt("tensor", t_ones)
    wt("tensor", t_bobf)
    t_BO = None
    for j in range(4):
        bk = pb[7]
        bk.gw(p, "tensor")
        tmm = op("tensor", lambda e, j=j, b_=bk: e.matmul(
            b_.ap[:, :], onesb[0:1, :], bo_bf[0:1, j * 512:(j + 1) * 512],
            start=True, stop=True), inc=True)
        bk.wr = tmm
        bk.gr(p, "scalar")
        t_BO = op("scalar", lambda e, j=j, b_=bk: e.copy(
            BO[:, j * 512:(j + 1) * 512], b_.ap[:, :]), inc=True)
        bk.rds.append(t_BO)

    cast_tok = {}
    img_tok = {}
    _ti = [0]

    def cast_steps(name, pieces=1):
        rows = w_in[name].shape[0]
        steps = []
        for q in range(pieces):
            r0, r1 = rows * q // pieces, rows * (q + 1) // pieces

            def f(r0=r0, r1=r1, name=name):
                cast_tok[name] = dma(dqg, w_bf[name][r0:r1, :],
                                     w_in[name][r0:r1, :])
            steps.append(f)
        return steps

    def img_steps(name):
        rows = w_in[name].shape[0]
        nspan = rows // 512
        steps = []

        def mk(i, s, name=name):
            def f():
                bidx = _ti[0] % 4
                _ti[0] += 1
                wt("sync", cast_tok[name])
                wt("sync", U["tbuf"][bidx])
                ttr = dmat(tbuf[bidx][:],
                           w_bf[name][s * 512:(s + 1) * 512,
                                      i * 128:(i + 1) * 128])
                wt("scalar", ttr)
                stt = dma(dqa, img[name][i, s], tbuf[bidx][:])
                U["tbuf"][bidx] = stt
                img_tok[name] = stt
            return f
        for i in range(16):
            for s in range(nspan):
                steps.append(mk(i, s))
        return steps

    fold_state = {"wt2r_user": None, "fstg_tok": [None, None],
                  "w2mix_last": None, "fi": 0}

    def fold_step(j):
        def f():
            wt("scalar", fold_state["wt2r_user"])
            wt("scalar", cast_tok["w2"])
            ldk = dma(dqa, wt2r[:].rearrange("t (a c) -> t a c", c=128),
                      w_bf["w2"].rearrange("d (t q) -> t d q", q=128)
                      [:, j * 32:(j + 1) * 32, :])
            last_mm = None
            for m in range(8):
                bk = pb[6 + (m % 2)]
                bk.gw(p, "tensor")
                wt("tensor", ldk)
                wt("tensor", t_hm)
                tmm = op("tensor", lambda e, b_=bk, m=m: e.matmul(
                    b_.ap[0:64, :], hmat[0:64, 0:64],
                    wt2r[:, m * 512:(m + 1) * 512], start=True, stop=True),
                    inc=True)
                bk.wr = tmm
                last_mm = tmm
                sidx = fold_state["fi"] % 2
                fold_state["fi"] += 1
                eng = "scalar" if m % 2 == 0 else "vector"
                wt(eng, fold_state["fstg_tok"][sidx])
                bk.gr(p, eng)
                if eng == "scalar":
                    trd = op("scalar", lambda e, s=sidx, b_=bk: e.copy(
                        fstg[s][:, :], b_.ap[0:64, :]), inc=True)
                else:
                    trd = op("vector", lambda e, s=sidx, b_=bk: e.tensor_copy(
                        fstg[s][:, :], b_.ap[0:64, :]), inc=True)
                bk.rds.append(trd)
                wt("gpsimd", trd)
                stt = dma(dqg, w2mix[:, j * 32 + m * 4:j * 32 + (m + 1) * 4, :]
                          .rearrange("t d q -> t (d q)"), fstg[sidx][:, :])
                fold_state["fstg_tok"][sidx] = stt
                fold_state["w2mix_last"] = stt
            fold_state["wt2r_user"] = last_mm
        return f

    def w2img_step(J):
        def f():
            for ds in range(4):
                bidx = _ti[0] % 4
                _ti[0] += 1
                wt("sync", fold_state["w2mix_last"])
                wt("sync", U["tbuf"][bidx])
                ttr = dmat(tbuf[bidx][:], w2mix[J, ds * 512:(ds + 1) * 512, :])
                wt("scalar", ttr)
                stt = dma(dqa, img["w2"][J, ds], tbuf[bidx][:])
                U["tbuf"][bidx] = stt
                img_tok["w2"] = stt
            return
        return f

    prep = (cast_steps("wk") + cast_steps("wv") + img_steps("wk")
            + img_steps("wv") + cast_steps("wq") + cast_steps("wo")
            + img_steps("wq") + img_steps("wo") + cast_steps("w2", 4)
            + [fold_step(j) for j in range(64)])
    prep_late = (cast_steps("w1", 4) + cast_steps("w3", 4) + img_steps("w1")
                 + img_steps("w3") + [w2img_step(J) for J in range(64)])
    prep_i = {"prep": 0, "late": 0}

    def drain(which, n):
        lst = prep if which == "prep" else prep_late
        k = 0
        while prep_i[which] < len(lst) and k < n:
            lst[prep_i[which]]()
            prep_i[which] += 1
            k += 1

    def layer_norm(src_ap, dst_ap, parity, in_tok):
        so = 32 * parity
        wt("vector", in_tok)
        wt("vector", U["stat"][parity])
        tprev = None
        for q in range(4):
            tprev = op("vector", lambda e, q=q, so=so, s_=src_ap: e.bn_stats(
                statb[:, so + 6 * q:so + 6 * q + 6],
                s_[:, q * 512:(q + 1) * 512]), inc=(q == 3))
        wt("vector", tprev)
        t_ag = op("vector", lambda e, so=so: e.bn_aggr(
            statb[:, so + 24:so + 26], statb[:, so + 0:so + 24]), inc=True)
        wt("scalar", t_ag)
        t_ln = op("scalar", lambda e, so=so: e.activation(
            statb[:, so + 26:so + 27], statb[:, so + 25:so + 26], AF.Ln,
            bias=1e-5), inc=True)
        wt("scalar", t_ln)
        t_rs = op("scalar", lambda e, so=so: e.activation(
            statb[:, so + 27:so + 28], statb[:, so + 26:so + 27], AF.Exp,
            scale=-0.5), inc=True)
        wt("vector", t_rs)
        t_mr = op("vector", lambda e, so=so: e.tensor_tensor(
            statb[:, so + 28:so + 29], statb[:, so + 24:so + 25],
            statb[:, so + 27:so + 28], ALU.mult), inc=True)
        wt("vector", t_mr)
        t_xn = op("vector", lambda e, so=so, s_=src_ap, d_=dst_ap:
                  e.tensor_scalar(d_, s_, statb[:, so + 27:so + 28],
                                  statb[:, so + 28:so + 29], ALU.mult,
                                  ALU.subtract), inc=True)
        U["stat"][parity] = t_xn
        return t_xn

    def phase_ln1(b):
        last = None
        for i in range(NTT):
            pr = i % 2
            wt("scalar", U["xb"][pr])
            tk = dma(dqa, xb[pr][:], x_in[b, i * 128:(i + 1) * 128, :])
            wt("vector", U["xnst"][pr])
            t_xn = layer_norm(xb[pr][:], xnst[pr][:], pr, tk)
            U["xb"][pr] = t_xn
            wt("gpsimd", t_xn)
            last = dma(dqg, xn1_s[b, i * 128:(i + 1) * 128, :], xnst[pr][:])
            U["xnst"][pr] = last
        return last

    def load_xnt(buf, src, b, c, pr, users, pre=None):
        wt("sync", users[pr])
        wt("sync", pre)
        tok = None
        for i in range(16):
            tok = dmat(buf[pr][:, i * 512:(i + 1) * 512],
                       src[b, c * 512:(c + 1) * 512, i * 128:(i + 1) * 128])
        return tok

    def load_slab(name, s, spr):
        wt("scalar", U["slab"][spr])
        wt("scalar", img_tok[name])
        return dma(dqa, slab[spr][:].rearrange("q (i c) -> q i c", c=512),
                   img[name][:, s].rearrange("i p c -> p i c"))

    def rope(ppr, col0, t_ev):
        rt = prow[ppr]
        wt("vector", t_ev)
        wt("vector", t_cos)
        wt("vector", t_sin)
        t4 = None
        for (d0, s0) in ((0, 32), (32, 0), (64, 96), (96, 64)):
            t4 = op("vector", lambda e, d0=d0, s0=s0, rt=rt, col0=col0:
                    e.tensor_tensor(ropec[d0:d0 + 32, :], rt[s0:s0 + 32, :],
                                    sintb[s0:s0 + 32, col0:col0 + QC],
                                    ALU.mult), inc=(d0 == 96))
        t_qc = op("vector", lambda e, rt=rt, col0=col0: e.tensor_tensor(
            rt[:], rt[:], costb[:, col0:col0 + QC], ALU.mult), inc=True)
        wt("vector", t4)
        wt("vector", t_qc)
        return op("vector", lambda e, rt=rt: e.tensor_tensor(
            rt[:], rt[:], ropec[:], ALU.add), inc=True)

    def proj_tiles(bank_sel, name, s, spr, cpr, xk, sl, p_major):
        pass

    def phase_kv(b, xn1_tok):
        k_last = None
        v_last = None
        for c in range(NCH):
            cpr = c % 2
            xk = load_xnt(xnt, xn1_s, b, c, cpr, U["xnt"], pre=xn1_tok)
            for s in range(4):
                spr = s % 2
                sl = load_slab("wk", s, spr)
                for J in range(4 * s, 4 * s + 4):
                    bk = pb[J % 2]
                    bk.gw(p, "tensor")
                    wt("tensor", xk)
                    wt("tensor", sl)
                    tmm = None
                    for i in range(16):
                        tmm = op("tensor", lambda e, b_=bk, i=i, J=J, s=s,
                                 spr=spr, cpr=cpr: e.matmul(
                            b_.ap[:, :],
                            slab[spr][:, i * 512 + (J - 4 * s) * 128:
                                      i * 512 + (J - 4 * s) * 128 + 128],
                            xnt[cpr][:, i * 512:(i + 1) * 512],
                            start=(i == 0), stop=(i == 15)), inc=(i == 15))
                    bk.wr = tmm
                    U["slab"][spr] = tmm
                    U["xnt"][cpr] = tmm
                    ppr = J % 2
                    bk.gr(p, "scalar")
                    wt("scalar", U["prow"][ppr])
                    t_ev = op("scalar", lambda e, b_=bk, ppr=ppr: e.copy(
                        prow[ppr][:], b_.ap[:, :]), inc=True)
                    bk.rds.append(t_ev)
                    t_rp = rope(ppr, c * QC, t_ev)
                    wt("gpsimd", t_rp)
                    k_last = dma(dqg, kT_s[b, J * 128:(J + 1) * 128,
                                           c * QC:(c + 1) * QC], prow[ppr][:])
                    U["prow"][ppr] = k_last
            for s in range(4):
                spr = s % 2
                sl = load_slab("wv", s, spr)
                for tt in range(4):
                    bk = pb[2 + tt % 2]
                    bk.gw(p, "tensor")
                    wt("tensor", xk)
                    wt("tensor", sl)
                    tmm = None
                    for i in range(16):
                        tmm = op("tensor", lambda e, b_=bk, i=i, tt=tt,
                                 spr=spr, cpr=cpr: e.matmul(
                            b_.ap[:, :],
                            xnt[cpr][:, i * 512 + tt * 128:
                                      i * 512 + tt * 128 + 128],
                            slab[spr][:, i * 512:(i + 1) * 512],
                            start=(i == 0), stop=(i == 15)), inc=(i == 15))
                    bk.wr = tmm
                    U["slab"][spr] = tmm
                    U["xnt"][cpr] = tmm
                    vpr = tt % 2
                    bk.gr(p, "scalar")
                    wt("scalar", U["vstg"][vpr])
                    t_ev = op("scalar", lambda e, b_=bk, vpr=vpr: e.copy(
                        vstg[vpr][:], b_.ap[:, :]), inc=True)
                    bk.rds.append(t_ev)
                    wt("gpsimd", t_ev)
                    v_last = dma(dqg, va_s[b, (c * 4 + tt) * 128:
                                           (c * 4 + tt) * 128 + 128,
                                           s * 512:(s + 1) * 512], vstg[vpr][:])
                    U["vstg"][vpr] = v_last
        return k_last, v_last

    drain("prep", len(prep))
    xn1_done = {}
    kv_done = {}
    xn1_done[0] = phase_ln1(0)
    kv_done[0] = phase_kv(0, xn1_done[0])
    drain("late", 150)
    xn1_done[1] = phase_ln1(1)
    kv_done[1] = phase_kv(1, xn1_done[1])
    drain("late", 150)
    barrier()
    init_scope.close()

    att = ExitStack()

    def sba(name, shape, dt=BF16):
        return sb(name, shape, dt, scope=att)
    ebuf = sba("ebuf", [128, 2 * 6 * 512])
    OT = sba("OT", [128, 16 * 512])
    QTc = sba("QTc", [128, 16 * 512])
    kbuf = [sba(f"kbuf{i}", [128, 2048]) for i in range(2)]
    vbuf = [sba(f"vbuf{i}", [128, 16 * 130]) for i in range(2)]
    oub = [sba(f"oub{i}", [64, 512]) for i in range(2)]
    rbf = [sba(f"rbf{i}", [1, 512]) for i in range(2)]
    x1p = [sb(f"x1p{i}", [128, 512], F32, scope=att) for i in range(2)]
    x1r = sb("x1r", [128, D], F32, scope=att)

    def eoff(hh, slot):
        return (hh * 6 + slot) * 512

    stair = []
    for hh in range(2):
        for j in range(1, 4):
            stair.append(op("vector", lambda e, hh=hh, j=j: e.memset(
                ebuf[:, eoff(hh, j):eoff(hh, j) + 128 * j], 0.0), inc=True))

    AU = {"kbuf": [None, None], "vbuf": [None, None], "qtc": None,
          "ebuf": {}, "oub": [None, None], "rbf": [None, None],
          "x1p": [None, None], "x1r": None}

    def phase_attn_chunk(b, c, cpr, xk):
        q0 = c * QC
        qt_last = None
        for s in range(4):
            spr = s % 2
            sl = load_slab("wq", s, spr)
            for J in range(4 * s, 4 * s + 4):
                bk = pb[J % 2]
                bk.gw(p, "tensor")
                wt("tensor", xk)
                wt("tensor", sl)
                wt("tensor", AU["qtc"])
                tmm = None
                for i in range(16):
                    tmm = op("tensor", lambda e, b_=bk, i=i, J=J, s=s, spr=spr,
                             cpr=cpr: e.matmul(
                        b_.ap[:, :],
                        slab[spr][:, i * 512 + (J - 4 * s) * 128:
                                  i * 512 + (J - 4 * s) * 128 + 128],
                        xnt[cpr][:, i * 512:(i + 1) * 512],
                        start=(i == 0), stop=(i == 15)), inc=(i == 15))
                bk.wr = tmm
                U["slab"][spr] = tmm
                U["xnt"][cpr] = tmm
                ppr = J % 2
                bk.gr(p, "scalar")
                wt("scalar", U["prow"][ppr])
                t_ev = op("scalar", lambda e, b_=bk, ppr=ppr: e.copy(
                    prow[ppr][:], b_.ap[:, :]), inc=True)
                bk.rds.append(t_ev)
                t_rp = rope(ppr, c * QC, t_ev)
                wt("vector", t_rp)
                t_cp = op("vector", lambda e, J=J, ppr=ppr: e.tensor_copy(
                    QTc[:, J * 512:(J + 1) * 512], prow[ppr][:]), inc=True)
                U["prow"][ppr] = t_cp
                qt_last = t_cp

        ot_last = None
        last_s_mm = None
        for hp in range(16):
            hpr = hp % 2
            glob = (hp == 15)
            kmin = 0 if (glob or c < 2) else 512
            kend = q0 + 512
            nkt = (kend - kmin) // 128
            full_k0 = list(range(kmin, q0, 128))
            stra_k0 = [q0 + 128 * j for j in range(4)]
            wt("gpsimd", AU["kbuf"][hpr])
            wt("gpsimd", kv_done[b][0])
            kb = dma(dqg, kbuf[hpr][:, 0:kend - kmin],
                     kT_s[b, hp * 128:(hp + 1) * 128, kmin:kend])
            wt("gpsimd", AU["vbuf"][hpr])
            wt("gpsimd", kv_done[b][1])
            for m in range(2):
                vb = dma(dqg, vbuf[hpr][:, 0:nkt * 130]
                         .rearrange("q (n h c) -> q n h c", h=2, c=65)
                         [:, :, m, 0:64],
                         va_s[b, kmin:kend,
                              hp * 128 + m * 64:hp * 128 + m * 64 + 64]
                         .rearrange("(n q) c -> q n c", q=128))
            wt("vector", vb)
            t_vo = op("vector", lambda e, hpr=hpr, nkt=nkt: e.memset(
                vbuf[hpr][:, 0:nkt * 130]
                .rearrange("q (n h c) -> q n h c", h=2, c=65)[:, :, :, 64:65],
                1.0), inc=True)

            exp_tok = {}
            for j, k0 in enumerate(stra_k0):
                for hh in range(2):
                    bk = pb[2 * (j % 2) + hh]
                    bk.gw(p, "tensor")
                    wt("tensor", kb)
                    wt("tensor", qt_last)
                    tmm = op("tensor", lambda e, b_=bk, hh=hh, k0=k0, hpr=hpr,
                             kmin=kmin, hp=hp: e.matmul(
                        b_.ap[:, :],
                        kbuf[hpr][hh * 64:hh * 64 + 64,
                                  k0 - kmin:k0 - kmin + 128],
                        QTc[hh * 64:hh * 64 + 64, hp * 512:(hp + 1) * 512],
                        start=True, stop=True, tile_position=(hh * 64, 0)),
                        inc=True)
                    bk.wr = tmm
                    last_s_mm = tmm
                    bk.gr(p, "scalar")
                    wt("scalar", AU["ebuf"].get((hh, j)))
                    t_ex = op("scalar", lambda e, b_=bk, hh=hh, j=j:
                              e.activation(
                        ebuf[:, eoff(hh, j) + 128 * j:eoff(hh, j) + 512],
                        b_.ap[:, 128 * j:512], AF.Exp, scale=SCL, bias=-SHIFT),
                        inc=True)
                    bk.rds.append(t_ex)
                    exp_tok[(hh, j)] = t_ex
            band = []
            for hh in range(2):
                for j in range(4):
                    wt("vector", exp_tok[(hh, j)])
                wt("vector", t_tri)
                for tz in stair:
                    wt("vector", tz)
                band.append(op("vector", lambda e, hh=hh: e.tensor_tensor(
                    ebuf[:, eoff(hh, 0):eoff(hh, 0) + 4 * 640]
                    .rearrange("q (j c) -> q j c", c=640)[:, :, 0:128],
                    ebuf[:, eoff(hh, 0):eoff(hh, 0) + 4 * 640]
                    .rearrange("q (j c) -> q j c", c=640)[:, :, 0:128],
                    tri4[:].rearrange("q (j c) -> q j c", c=128), ALU.mult),
                    inc=True))

            seq = [("f", k0) for k0 in full_k0] + [("s", k0) for k0 in stra_k0]
            o_tok = [None, None]
            for idx, (kind, k0) in enumerate(seq):
                first = (idx == 0)
                lastt = (idx == len(seq) - 1)
                if kind == "f":
                    slot = 4 + (idx % 2)
                    for hh in range(2):
                        bk = pb[2 * (idx % 2) + hh]
                        bk.gw(p, "tensor")
                        wt("tensor", kb)
                        wt("tensor", qt_last)
                        tmm = op("tensor", lambda e, b_=bk, hh=hh, k0=k0,
                                 hpr=hpr, kmin=kmin, hp=hp: e.matmul(
                            b_.ap[:, :],
                            kbuf[hpr][hh * 64:hh * 64 + 64,
                                      k0 - kmin:k0 - kmin + 128],
                            QTc[hh * 64:hh * 64 + 64, hp * 512:(hp + 1) * 512],
                            start=True, stop=True, tile_position=(hh * 64, 0)),
                            inc=True)
                        bk.wr = tmm
                        last_s_mm = tmm
                        bk.gr(p, "scalar")
                        wt("scalar", AU["ebuf"].get((hh, slot)))
                        t_ex = op("scalar", lambda e, b_=bk, hh=hh, slot=slot:
                                  e.activation(
                            ebuf[:, eoff(hh, slot):eoff(hh, slot) + 512],
                            b_.ap[:, :], AF.Exp, scale=SCL, bias=-SHIFT),
                            inc=True)
                        bk.rds.append(t_ex)
                        exp_tok[(hh, slot)] = t_ex
                else:
                    slot = (k0 - q0) // 128
                for hh in range(2):
                    obk = pb[4 + hh]
                    if first:
                        obk.gw(p, "tensor")
                    if kind == "f":
                        wt("tensor", exp_tok[(hh, slot)])
                    else:
                        wt("tensor", band[hh])
                    wt("tensor", vb)
                    wt("tensor", t_vo)
                    kti = (k0 - kmin) // 128
                    tmo = op("tensor", lambda e, b_=obk, hh=hh, kti=kti,
                             hpr=hpr, slot=slot, first=first, lastt=lastt:
                             e.matmul(
                        b_.ap[0:65, :],
                        vbuf[hpr][:, kti * 130 + hh * 65:
                                  kti * 130 + hh * 65 + 65],
                        ebuf[:, eoff(hh, slot):eoff(hh, slot) + 512],
                        start=first, stop=lastt), inc=True)
                    o_tok[hh] = tmo
                    if kind == "f":
                        AU["ebuf"][(hh, slot)] = tmo
            pb[4].wr = o_tok[0]
            pb[5].wr = o_tok[1]
            AU["kbuf"][hpr] = o_tok[1]
            AU["vbuf"][hpr] = o_tok[1]
            for j in range(4):
                AU["ebuf"][(0, j)] = o_tok[0]
                AU["ebuf"][(1, j)] = o_tok[1]

            for hh in range(2):
                bk = pb[4 + hh]
                bk.gr(p, "vector")
                wt("vector", AU["rbf"][hh])
                def _rcp(e, b_=bk, hh=hh):
                    with nc.allow_low_precision(
                            reason="1/n row in bf16 feeds a bf16 broadcast "
                                   "matmul; 2^-9 uniform scale error"):
                        return e.reciprocal(rbf[hh][:], b_.ap[64:65, :])
                t_rc = op("vector", _rcp, inc=True)
                bk.gr(p, "scalar")
                wt("scalar", AU["oub"][hh])
                t_ou = op("scalar", lambda e, b_=bk, hh=hh: e.copy(
                    oub[hh][:], b_.ap[0:64, :]), inc=True)
                bk.rds.append(t_rc)
                bk.rds.append(t_ou)
                rbk = pb[6 + hh]
                rbk.gw(p, "tensor")
                wt("tensor", t_rc)
                wt("tensor", t_ones)
                t_rb = op("tensor", lambda e, b_=rbk, hh=hh: e.matmul(
                    b_.ap[0:64, :], onesb[0:1, 0:64], rbf[hh][:],
                    start=True, stop=True), inc=True)
                rbk.wr = t_rb
                AU["rbf"][hh] = t_rb
                rbk.gr(p, "vector")
                wt("vector", t_ou)
                t_ot = op("vector", lambda e, b_=rbk, hh=hh, hp=hp:
                          e.tensor_tensor(
                    OT[hh * 64:hh * 64 + 64, hp * 512:(hp + 1) * 512],
                    oub[hh][:], b_.ap[0:64, :], ALU.mult), inc=True)
                rbk.rds.append(t_ot)
                AU["oub"][hh] = t_ot
                ot_last = t_ot
        AU["qtc"] = last_s_mm
        return ot_last

    def phase_wo(b, c, ot_last):
        x1_tok = {}
        for s in range(4):
            spr = s % 2
            sl = load_slab("wo", s, spr)
            for tau in range(4):
                trow = c * 512 + tau * 128
                bk = pb[tau]
                bk.gw(p, "tensor")
                wt("tensor", sl)
                wt("tensor", ot_last)
                tmm = None
                for i in range(16):
                    tmm = op("tensor", lambda e, b_=bk, i=i, tau=tau, spr=spr:
                             e.matmul(
                        b_.ap[:, :],
                        OT[:, i * 512 + tau * 128:i * 512 + tau * 128 + 128],
                        slab[spr][:, i * 512:(i + 1) * 512],
                        start=(i == 0), stop=(i == 15)), inc=(i == 15))
                bk.wr = tmm
                U["slab"][spr] = tmm
                xpr = tau % 2
                wt("scalar", U["xb"][xpr])
                tkx = dma(dqa, xb[xpr][:, 0:512],
                          x_in[b, trow:trow + 128, s * 512:(s + 1) * 512])
                bk.gr(p, "vector")
                wt("vector", tkx)
                wt("vector", t_BO)
                wt("vector", AU["x1p"][xpr])
                t_a1 = op("vector", lambda e, b_=bk, s=s, xpr=xpr:
                          e.tensor_tensor(
                    x1p[xpr][:], b_.ap[:, :], BO[:, s * 512:(s + 1) * 512],
                    ALU.add), inc=True)
                wt("vector", t_a1)
                t_a2 = op("vector", lambda e, xpr=xpr: e.tensor_tensor(
                    x1p[xpr][:], x1p[xpr][:], xb[xpr][:, 0:512], ALU.add),
                    inc=True)
                bk.rds.append(t_a2)
                U["xb"][xpr] = t_a2
                wt("gpsimd", t_a2)
                stx = dma(dqg, x1_s[b, trow:trow + 128, s * 512:(s + 1) * 512],
                          x1p[xpr][:])
                AU["x1p"][xpr] = stx
                x1_tok[(s, tau)] = stx
        last_xn2 = None
        for tau in range(4):
            trow = c * 512 + tau * 128
            for s in range(4):
                wt("scalar", x1_tok[(s, tau)])
            wt("scalar", AU["x1r"])
            tk = dma(dqa, x1r[:], x1_s[b, trow:trow + 128, :])
            pr = tau % 2
            wt("vector", U["xnst"][pr])
            t_xn = layer_norm(x1r[:], xnst[pr][:], pr, tk)
            AU["x1r"] = t_xn
            wt("gpsimd", t_xn)
            last_xn2 = dma(dqg, xn2_s[b, trow:trow + 128, :], xnst[pr][:])
            U["xnst"][pr] = last_xn2
        x1_last = max(x1_tok.values(), key=lambda t: t[1])
        return x1_last, last_xn2

    x1_done = {}
    xn2_done = {}
    for b in range(B):
        for c in range(NCH):
            cpr = c % 2
            xk = load_xnt(xnt, xn1_s, b, c, cpr, U["xnt"], pre=xn1_done[b])
            ot_last = phase_attn_chunk(b, c, cpr, xk)
            x1_done[b], xn2_done[b] = phase_wo(b, c, ot_last)
        drain("late", 250)
    drain("late", len(prep_late))
    barrier()
    att.close()
    era.close()

    ffn = ExitStack()

    def sbf(name, shape, dt=BF16):
        return sb(name, shape, dt, scope=ffn)
    xnt2 = xnt
    zbuf = sbf("zbuf", [128, 64 * 512])
    fslab = [sbf(f"fslab{i}", [128, 32 * 512]) for i in range(2)]
    gbuf = [sbf(f"gbuf{i}", [128, 512]) for i in range(2)]
    w2b = [sbf(f"w2b{i}", [128, 8 * 512]) for i in range(2)]
    x1b = [sb(f"x1b{i}", [128, 512], F32, scope=ffn) for i in range(2)]
    x2b = [sb(f"x2b{i}", [128, 512], F32, scope=ffn) for i in range(2)]

    FU = {"xnt2": [None, None], "fslab": [None, None], "gbuf": [None, None],
          "w2b": [None, None], "x1b": [None, None], "x2b": [None, None],
          "zbuf": None, "w2tok": [None, None]}
    out_last = None
    for b in range(B):
        for c in range(NCH):
            cpr = c % 2
            xk = load_xnt(xnt2, xn2_s, b, c, cpr, FU["xnt2"], pre=xn2_done[b])
            z_writes = []
            for s in range(16):
                spr = s % 2
                wt("scalar", FU["fslab"][spr])
                wt("scalar", img_tok["w1"])
                wt("scalar", img_tok["w3"])
                t1 = dma(dqa, fslab[spr][:, 0:16 * 512]
                         .rearrange("q (i c) -> q i c", c=512),
                         img["w1"][:, s].rearrange("i p c -> p i c"))
                t3 = dma(dqa, fslab[spr][:, 16 * 512:32 * 512]
                         .rearrange("q (i c) -> q i c", c=512),
                         img["w3"][:, s].rearrange("i p c -> p i c"))
                for J in range(4 * s, 4 * s + 4):
                    jpr = J % 2
                    bka = pb[0 + 2 * jpr]
                    bkb = pb[1 + 2 * jpr]
                    for bk, woff, tw in ((bka, 0, t1), (bkb, 16 * 512, t3)):
                        bk.gw(p, "tensor")
                        wt("tensor", xk)
                        wt("tensor", tw)
                        tmm = None
                        for i in range(16):
                            tmm = op("tensor", lambda e, b_=bk, i=i, J=J, s=s,
                                     spr=spr, woff=woff, cpr=cpr: e.matmul(
                                b_.ap[:, :],
                                fslab[spr][:, woff + i * 512 +
                                           (J - 4 * s) * 128:
                                           woff + i * 512 +
                                           (J - 4 * s) * 128 + 128],
                                xnt2[cpr][:, i * 512:(i + 1) * 512],
                                start=(i == 0), stop=(i == 15)),
                                inc=(i == 15))
                        bk.wr = tmm
                        FU["fslab"][spr] = tmm
                        FU["xnt2"][cpr] = tmm
                    gpr = J % 2
                    bka.gr(p, "scalar")
                    wt("scalar", FU["gbuf"][gpr])
                    t_si = op("scalar", lambda e, b_=bka, gpr=gpr:
                              e.activation(gbuf[gpr][:], b_.ap[:, :], AF.Silu),
                              inc=True)
                    bka.rds.append(t_si)
                    bkb.gr(p, "vector")
                    wt("vector", t_si)
                    t_g = op("vector", lambda e, b_=bkb, gpr=gpr:
                             e.scalar_tensor_tensor(
                        gbuf[gpr][:], b_.ap[:, :], GSC, gbuf[gpr][:],
                        ALU.mult, ALU.mult), inc=True)
                    bkb.rds.append(t_g)
                    zbk = pb[4 + jpr]
                    zbk.gw(p, "tensor")
                    wt("tensor", t_g)
                    wt("tensor", t_hm)
                    t_z = op("tensor", lambda e, b_=zbk, gpr=gpr: e.matmul(
                        b_.ap[:, :], hmat[:], gbuf[gpr][:], start=True,
                        stop=True), inc=True)
                    zbk.wr = t_z
                    FU["gbuf"][gpr] = t_z
                    zbk.gr(p, "scalar")
                    if FU["zbuf"] is not None:
                        wt("scalar", FU["zbuf"])
                    t_ze = op("scalar", lambda e, b_=zbk, J=J: e.copy(
                        zbuf[:, J * 512:(J + 1) * 512], b_.ap[:, :]),
                        inc=True)
                    zbk.rds.append(t_ze)
                    z_writes.append(t_ze)

            last_mm = None
            for ds in range(4):
                banks = [pb[k] for k in range(4)]
                for tau in range(4):
                    banks[tau].gw(p, "tensor")
                for g in range(2):
                    wt("scalar", FU["w2b"][g])
                    wt("scalar", img_tok["w2"])
                    FU["w2tok"][g] = dma(
                        dqa, w2b[g][:].rearrange("q (j c) -> q j c", c=512),
                        img["w2"][g * 8:(g + 1) * 8, ds]
                        .rearrange("j p c -> p j c"))
                for J in range(64):
                    g = J // 8
                    gpr2 = g % 2
                    wt("tensor", FU["w2tok"][gpr2])
                    wt("tensor", z_writes[J])
                    grp_last = None
                    for tau in range(4):
                        bk = banks[tau]
                        tmm = op("tensor", lambda e, b_=bk, J=J, tau=tau,
                                 gpr2=gpr2: e.matmul(
                            b_.ap[:, :],
                            zbuf[:, J * 512 + tau * 128:
                                 J * 512 + tau * 128 + 128],
                            w2b[gpr2][:, (J % 8) * 512:(J % 8 + 1) * 512],
                            start=(J == 0), stop=(J == 63)),
                            inc=(J == 63 or tau == 3))
                        grp_last = tmm
                        if J == 63:
                            bk.wr = tmm
                            last_mm = tmm
                    if J % 8 == 7 and J < 48:
                        FU["w2b"][gpr2] = grp_last
                        wt("scalar", grp_last)
                        FU["w2tok"][gpr2] = dma(
                            dqa, w2b[gpr2][:]
                            .rearrange("q (j c) -> q j c", c=512),
                            img["w2"][(g + 2) * 8:(g + 3) * 8, ds]
                            .rearrange("j p c -> p j c"))
                for tau in range(4):
                    bk = banks[tau]
                    trow = c * 512 + tau * 128
                    xpr = tau % 2
                    wt("scalar", FU["x1b"][xpr])
                    p.wait("scalar", x1_done[b])
                    tk1 = dma(dqa, x1b[xpr][:],
                              x1_s[b, trow:trow + 128,
                                   ds * 512:(ds + 1) * 512])
                    bk.gr(p, "vector")
                    wt("vector", tk1)
                    wt("vector", FU["x2b"][xpr])
                    t_x2 = op("vector", lambda e, b_=bk, xpr=xpr:
                              e.tensor_tensor(
                        x2b[xpr][:], b_.ap[:, :], x1b[xpr][:], ALU.add),
                        inc=True)
                    bk.rds.append(t_x2)
                    FU["x1b"][xpr] = t_x2
                    wt("gpsimd", t_x2)
                    out_last = dma(dqg, out_t[b, trow:trow + 128,
                                              ds * 512:(ds + 1) * 512],
                                   x2b[xpr][:])
                    FU["x2b"][xpr] = out_last
            FU["zbuf"] = last_mm

    p.wait("sync", (dqg, dqg.n))
    p.wait("sync", (dqa, dqa.n))
    p.wait("sync", (dqs, dqs.n))
    ffn.close()

    with nc.Block() as blk:
        p.run(blk)
    es.close()
    return nc


def _tables():
    inv = 10000.0 ** (-(np.arange(0, HD, 2, dtype=np.float64) / HD))
    pos = np.arange(T, dtype=np.float64)
    fr = pos[None, :] * inv[:, None]
    cos32 = np.cos(fr).astype(np.float32)
    sin32 = np.sin(fr).astype(np.float32)
    cost = np.zeros((128, T), dtype=np.float32)
    sint = np.zeros((128, T), dtype=np.float32)
    for half in range(2):
        b0 = half * 64
        cost[b0:b0 + 32] = cos32
        cost[b0 + 32:b0 + 64] = cos32
        sint[b0:b0 + 32] = sin32
        sint[b0 + 32:b0 + 64] = -sin32
    qq = np.arange(128)[None, :]
    kk = np.arange(128)[:, None]
    tri = (qq >= kk).astype(np.float32).astype(BFD)
    pc = np.zeros((128, 128), dtype=np.int64)
    for i in range(128):
        for j in range(128):
            pc[i, j] = bin(i & j).count("1")
    hmat = np.where(pc % 2 == 0, 1.0, -1.0).astype(np.float32).astype(BFD)
    return cost, sint, tri, hmat


_NC_CACHE = None


def kernel(L0_attn_hidden_states, L0_attn_Wq, L0_attn_Wk, L0_attn_Wv, L0_attn_Wo,
           L0_attn_bo, L0_ffn_W1, L0_ffn_W2, L0_ffn_W3):
    global _NC_CACHE
    x = np.ascontiguousarray(L0_attn_hidden_states, dtype=np.float32)
    cost, sint, tri, hmat = _tables()
    if _NC_CACHE is None:
        _NC_CACHE = build_nc()
    nc = _NC_CACHE
    base = {
        "wq": np.ascontiguousarray(L0_attn_Wq, dtype=np.float32),
        "wk": np.ascontiguousarray(L0_attn_Wk, dtype=np.float32),
        "wv": np.ascontiguousarray(L0_attn_Wv, dtype=np.float32),
        "wo": np.ascontiguousarray(L0_attn_Wo, dtype=np.float32),
        "bo": np.ascontiguousarray(L0_attn_bo, dtype=np.float32).reshape(1, D),
        "w1": np.ascontiguousarray(L0_ffn_W1, dtype=np.float32),
        "w2": np.ascontiguousarray(L0_ffn_W2, dtype=np.float32),
        "w3": np.ascontiguousarray(L0_ffn_W3, dtype=np.float32),
        "cost_t": cost, "sint_t": sint,
        "tri_t": np.asarray(tri), "hmat_t": np.asarray(hmat),
    }
    in_maps = []
    for core in range(8):
        m = dict(base)
        m["x"] = x[core * 2:core * 2 + 2]
        in_maps.append(m)
    res = run_bass_kernel_spmd(nc, in_maps, core_ids=list(range(8)))
    return np.concatenate([r["out"] for r in res.results], axis=0)


# version 06
# speedup vs baseline: 1.1073x; 1.1073x over previous
import numpy as np
import ml_dtypes
import concourse.bass as bass
import concourse.mybir as mybir
from concourse.bass_utils import run_bass_kernel_spmd
from contextlib import ExitStack

F32 = mybir.dt.float32
BF16 = mybir.dt.bfloat16
AF = mybir.ActivationFunctionType
ALU = mybir.AluOpType
BFD = ml_dtypes.bfloat16

B, T, D, NH, HD, HID = 2, 2048, 2048, 32, 64, 8192
NTT, NCH, QC = 16, 4, 512
SHIFT = 8.0
SCL = 0.125
GSC = float(1.0 / np.sqrt(HID))

ENGS = ("sync", "tensor", "vector", "scalar", "gpsimd")


class Sem:
    __slots__ = ("h", "n", "key", "eng")

    def __init__(self, h, key):
        self.h, self.n, self.key = h, 0, key
        self.eng = None

    def bump(self, k):
        self.n += k
        return self.n


class Plan:
    def __init__(self):
        self.ops = {e: [] for e in ENGS}
        self.hwm = {}

    def emit(self, eng, fn):
        self.ops[eng].append(fn)

    def wait(self, eng, tok):
        if tok is None:
            return
        sem, val = tok
        if val <= 0:
            return
        k = (eng, sem.key)
        if self.hwm.get(k, -1) >= val:
            return
        self.hwm[k] = val
        self.emit(eng, lambda e, s=sem.h, v=val: e.wait_ge(s, v))

    def run(self, blk):
        for eng in ENGS:
            ops_ = self.ops[eng]

            def mk(o):
                def body(e):
                    for f in o:
                        f(e)
                return body
            getattr(blk, eng)(mk(ops_))


class Bank:
    def __init__(self, ap):
        self.ap = ap
        self.wr = None
        self.rds = []

    def gw(self, p, eng):
        for t in self.rds:
            p.wait(eng, t)
        self.rds = []

    def gr(self, p, eng):
        p.wait(eng, self.wr)


def build_nc():
    nc = bass.Bass("TRN2", target_bir_lowering=False)
    p = Plan()
    es = ExitStack()

    def din(name, shape, dt=F32):
        return nc.dram_tensor(name, shape, dt, kind="ExternalInput").ap()

    x_in = din("x", [B, T, D])
    w_in = {n: din(n, [D, D]) for n in ("wq", "wk", "wv", "wo")}
    bo_in = din("bo", [1, D])
    w_in["w1"] = din("w1", [D, HID])
    w_in["w2"] = din("w2", [HID, D])
    w_in["w3"] = din("w3", [D, HID])
    cost_in = din("cost_t", [128, T])
    sint_in = din("sint_t", [128, T])
    tri_in = din("tri_t", [128, 128], BF16)
    hmat_in = din("hmat_t", [128, 128], BF16)
    out_t = nc.dram_tensor("out", [B, T, D], F32, kind="ExternalOutput").ap()

    import os as _os
    _DBG = _os.environ.get("KDEBUG", "") == "1"
    x1d_t = (nc.dram_tensor("x1d", [B, T, D], F32, kind="ExternalOutput").ap()
             if _DBG else None)
    ktd_t = (nc.dram_tensor("ktd", [B, D, T], F32, kind="ExternalOutput").ap()
             if _DBG else None)
    vad_t = (nc.dram_tensor("vad", [B, T, D], F32, kind="ExternalOutput").ap()
             if _DBG else None)

    def scr(name, shape, dt=BF16):
        return nc.dram_tensor(name, shape, dt, kind="Internal").ap()

    w_bf = {n: scr(n + "_bf", list(w_in[n].shape))
            for n in ("wq", "wk", "wv", "wo", "w1", "w3")}
    w2fold = scr("w2fold", [HID, D])
    xn1_s = scr("xn1_s", [B, T, D])
    xn2_s = scr("xn2_s", [B, T, D])
    kT_s = scr("kT_s", [B, D, T])
    va_s = scr("va_s", [B, T, D])
    x1_s = scr("x1_s", [B, T, D], F32)

    sems = {}
    for n in ("spe", "ssc", "sdv", "sgp"):
        sems[n] = Sem(es.enter_context(nc.semaphore(n)), n)

    def lane(name, eng):
        s = Sem(es.enter_context(nc.semaphore(name)), name)
        s.eng = eng
        sems[name] = s
        return s

    lts = [lane("lts0", "sync"), lane("lts1", "sync")]
    lfs = [lane("lfs0", "sync"), lane("lfs1", "sync")]
    ldbg = lane("ldbg", "sync")
    lini = lane("lini", "scalar")
    lini2 = lane("lini2", "scalar")
    lslab = [lane("lslab0", "scalar"), lane("lslab1", "scalar")]
    lxb = [lane("lxb0", "scalar"), lane("lxb1", "scalar")]
    lx1r = lane("lx1r", "scalar")
    lw2b = [lane("lw2b0", "scalar"), lane("lw2b1", "scalar")]
    lstx = [lane("lstx0", "gpsimd"), lane("lstx1", "gpsimd")]
    lsty = [lane("lsty0", "gpsimd"), lane("lsty1", "gpsimd")]
    lktl = [lane("lktl0", "gpsimd"), lane("lktl1", "gpsimd")]
    lval = [lane("lval0", "gpsimd"), lane("lval1", "gpsimd")]
    lx1s = [lane("lx1s0", "gpsimd"), lane("lx1s1", "gpsimd")]
    lout = [lane("lout0", "gpsimd"), lane("lout1", "gpsimd")]
    lkb = [lane("lkb0", "gpsimd"), lane("lkb1", "gpsimd")]
    lvb = [lane("lvb0", "gpsimd"), lane("lvb1", "gpsimd")]
    lcast = {n: lane("lc_" + n, "gpsimd")
             for n in ("wk", "wv", "wq", "wo", "w1", "w3")}
    lfld = [lane("lfld0", "gpsimd"), lane("lfld1", "gpsimd")]
    lcdbg = lane("lcdbg", "gpsimd")
    spe, ssc, sdv = sems["spe"], sems["ssc"], sems["sdv"]
    ESEM = {"tensor": spe, "scalar": ssc, "vector": sdv, "gpsimd": sems["sgp"]}

    def dma(q, out_ap, in_ap):
        v = q.bump(16)
        p.emit(q.eng, lambda e, o=out_ap, i=in_ap, sh=q.h:
               e.dma_start(o, i).then_inc(sh, 16))
        return (q, v)

    def dmat(q, out_ap, in_ap):
        v = q.bump(16)
        p.emit("sync", lambda e, o=out_ap, i=in_ap, sh=q.h:
               e.dma_start_transpose(o, i).then_inc(sh, 16))
        return (q, v)

    def op(eng, fn, inc=False):
        if inc:
            s = ESEM[eng]
            v = s.bump(1)
            p.emit(eng, lambda e, f=fn, sh=s.h: f(e).then_inc(sh, 1))
            return (s, v)
        p.emit(eng, fn)
        return None

    def wt(eng, tok):
        if isinstance(tok, list):
            for t in tok:
                p.wait(eng, t)
        else:
            p.wait(eng, tok)

    def barrier():
        toks = [(s, s.n) for s in sems.values()]
        for eng in ENGS:
            for t in toks:
                p.wait(eng, t)

    def sb(name, shape, dt=F32, scope=None):
        return (scope or es).enter_context(nc.sbuf_tensor(name, shape, dt))

    cpool = sb("cpool", [128, 4])
    hmat = sb("hmat", [128, 128], BF16)
    onesb = sb("onesb", [1, 128], BF16)
    xnt = [sb(f"xnt{i}", [128, 16 * 512], BF16) for i in range(2)]
    era = ExitStack()
    costb = sb("costb", [128, T], BF16, scope=era)
    sintb = sb("sintb", [128, T], BF16, scope=era)
    tri4 = sb("tri4", [128, 512], BF16, scope=era)
    BO = sb("BO", [128, D], scope=era)
    statb = sb("statb", [128, 64], scope=era)
    xb = [sb(f"xb{i}", [128, D], scope=era) for i in range(2)]
    xnst = [sb(f"xnst{i}", [128, D], BF16, scope=era) for i in range(2)]
    slab = [sb(f"slab{i}", [128, 16 * 512], BF16, scope=era) for i in range(2)]
    prow = [sb(f"prow{i}", [128, 512], BF16, scope=era) for i in range(2)]
    vstg = [sb(f"vstg{i}", [128, 512], BF16, scope=era) for i in range(2)]
    ropec = sb("ropec", [128, 512], BF16, scope=era)
    w2ld = [sb(f"w2ld{i}", [64, D], BF16, scope=era) for i in range(2)]
    fstg = [sb(f"fstg{i}", [64, 512], BF16, scope=era) for i in range(2)]

    pb = [Bank(es.enter_context(nc.psum_tensor(f"pb{i}", [128, 512], F32)))
          for i in range(8)]

    U = {"xnt": [None, None], "slab": [None, None], "xb": [None, None],
         "xnst": [None, None], "prow": [None, None],
         "stat": [None, None], "vstg": [None, None]}

    op("vector", lambda e: e.memset(cpool[:, 0:1], -SHIFT))
    op("vector", lambda e: e.memset(cpool[:, 1:2], 1e-5))
    t_ones = op("vector", lambda e: e.memset(onesb[0:1, :], 1.0), inc=True)
    nc.const_aps.register(F32, -SHIFT, cpool[:, 0:1])
    nc.const_aps.register(F32, 1e-5, cpool[:, 1:2])

    init_scope = ExitStack()
    tblf = sb("tblf", [128, T], scope=init_scope)
    bo_st = sb("bo_st", [1, D], scope=init_scope)
    bo_bf = sb("bo_bf", [1, D], BF16, scope=init_scope)

    tk = dma(lini, tblf[:], cost_in)
    wt("vector", tk)
    t_cos = op("vector", lambda e: e.tensor_copy(costb[:], tblf[:]), inc=True)
    wt("scalar", t_cos)
    tk = dma(lini, tblf[:], sint_in)
    wt("vector", tk)
    t_sin = op("vector", lambda e: e.tensor_copy(sintb[:], tblf[:]), inc=True)
    wt("scalar", t_sin)
    tk = dma(lini, tri4[:, 0:128], tri_in)
    wt("vector", tk)
    t_tri = None
    for r in range(1, 4):
        wt("vector", t_tri)
        t_tri = op("vector", lambda e, r=r: e.tensor_copy(
            tri4[:, r * 128:(r + 1) * 128], tri4[:, 0:128]), inc=True)
    t_hm = dma(lini2, hmat[:], hmat_in)
    wt("scalar", t_tri)
    tk = dma(lini, bo_st[:], bo_in)
    wt("vector", tk)
    t_bobf = op("vector", lambda e: e.tensor_copy(bo_bf[:], bo_st[:]), inc=True)
    wt("tensor", t_ones)
    wt("tensor", t_bobf)
    t_BO = None
    for j in range(4):
        bk = pb[7]
        bk.gw(p, "tensor")
        tmm = op("tensor", lambda e, j=j, b_=bk: e.matmul(
            b_.ap[:, :], onesb[0:1, :], bo_bf[0:1, j * 512:(j + 1) * 512],
            start=True, stop=True), inc=True)
        bk.wr = tmm
        bk.gr(p, "scalar")
        t_BO = op("scalar", lambda e, j=j, b_=bk: e.copy(
            BO[:, j * 512:(j + 1) * 512], b_.ap[:, :]), inc=True)
        bk.rds.append(t_BO)

    cast_tok = {}

    def cast_steps(name, pieces=1):
        rows = w_in[name].shape[0]
        steps = []
        for q in range(pieces):
            r0, r1 = rows * q // pieces, rows * (q + 1) // pieces

            def f(r0=r0, r1=r1, name=name):
                cast_tok[name] = dma(lcast[name], w_bf[name][r0:r1, :],
                                     w_in[name][r0:r1, :])
            steps.append(f)
        return steps

    fold_state = {"ld": [None, None], "fstg": [None, None], "last": None}

    def fold_step(q):
        def f():
            lpr = q % 2
            wt("gpsimd", fold_state["ld"][lpr])
            ldk = dma(lfld[lpr], w2ld[lpr][:, :],
                      w_in["w2"].rearrange("(t r) d -> t r d", r=128)[:, q, :])
            last_mm = None
            for ds in range(4):
                bk = pb[6 + (ds % 2)]
                bk.gw(p, "tensor")
                wt("tensor", ldk)
                wt("tensor", t_hm)
                tmm = op("tensor", lambda e, b_=bk, ds=ds, lpr=lpr: e.matmul(
                    b_.ap[0:64, :], hmat[0:64, 0:64],
                    w2ld[lpr][:, ds * 512:(ds + 1) * 512],
                    start=True, stop=True), inc=True)
                bk.wr = tmm
                last_mm = tmm
                sidx = ds % 2
                eng = "scalar" if ds % 2 == 0 else "vector"
                wt(eng, fold_state["fstg"][sidx])
                bk.gr(p, eng)
                if eng == "scalar":
                    trd = op("scalar", lambda e, s=sidx, b_=bk: e.copy(
                        fstg[s][:, :], b_.ap[0:64, :]), inc=True)
                else:
                    trd = op("vector", lambda e, s=sidx, b_=bk: e.tensor_copy(
                        fstg[s][:, :], b_.ap[0:64, :]), inc=True)
                bk.rds.append(trd)
                wt("sync", trd)
                stt = dma(lfs[sidx], w2fold.rearrange("(t r) d -> t r d", r=128)
                          [:, q, ds * 512:(ds + 1) * 512], fstg[sidx][:, :])
                fold_state["fstg"][sidx] = stt
                fold_state["last"] = stt
            fold_state["ld"][lpr] = last_mm
        return f

    prep = (cast_steps("wk") + cast_steps("wv") + cast_steps("wq")
            + cast_steps("wo"))
    prep_late = (cast_steps("w1", 4) + cast_steps("w3", 4)
                 + [fold_step(q) for q in range(128)])
    prep_i = {"prep": 0, "late": 0}

    def drain(which, n):
        lst = prep if which == "prep" else prep_late
        k = 0
        while prep_i[which] < len(lst) and k < n:
            lst[prep_i[which]]()
            prep_i[which] += 1
            k += 1

    def layer_norm(src_ap, dst_ap, parity, in_tok):
        so = 32 * parity
        wt("vector", in_tok)
        wt("vector", U["stat"][parity])
        tprev = None
        for q in range(4):
            tprev = op("vector", lambda e, q=q, so=so, s_=src_ap: e.bn_stats(
                statb[:, so + 6 * q:so + 6 * q + 6],
                s_[:, q * 512:(q + 1) * 512]), inc=(q == 3))
        wt("vector", tprev)
        t_ag = op("vector", lambda e, so=so: e.bn_aggr(
            statb[:, so + 24:so + 26], statb[:, so + 0:so + 24]), inc=True)
        wt("scalar", t_ag)
        t_ln = op("scalar", lambda e, so=so: e.activation(
            statb[:, so + 26:so + 27], statb[:, so + 25:so + 26], AF.Ln,
            bias=1e-5), inc=True)
        wt("scalar", t_ln)
        t_rs = op("scalar", lambda e, so=so: e.activation(
            statb[:, so + 27:so + 28], statb[:, so + 26:so + 27], AF.Exp,
            scale=-0.5), inc=True)
        wt("vector", t_rs)
        t_mr = op("vector", lambda e, so=so: e.tensor_tensor(
            statb[:, so + 28:so + 29], statb[:, so + 24:so + 25],
            statb[:, so + 27:so + 28], ALU.mult), inc=True)
        wt("vector", t_mr)
        t_xn = op("vector", lambda e, so=so, s_=src_ap, d_=dst_ap:
                  e.tensor_scalar(d_, s_, statb[:, so + 27:so + 28],
                                  statb[:, so + 28:so + 29], ALU.mult,
                                  ALU.subtract), inc=True)
        U["stat"][parity] = t_xn
        return t_xn

    def phase_ln1(b):
        for i in range(NTT):
            pr = i % 2
            wt("scalar", U["xb"][pr])
            tk = dma(lxb[pr], xb[pr][:], x_in[b, i * 128:(i + 1) * 128, :])
            wt("vector", U["xnst"][pr])
            t_xn = layer_norm(xb[pr][:], xnst[pr][:], pr, tk)
            U["xb"][pr] = t_xn
            wt("gpsimd", t_xn)
            last = dma(lstx[pr], xn1_s[b, i * 128:(i + 1) * 128, :],
                       xnst[pr][:])
            U["xnst"][pr] = last
        return [(s, s.n) for s in lstx]

    def load_xnt(buf, src, b, c, pr, users, pre=None):
        wt("sync", users[pr])
        wt("sync", pre)
        tok = None
        for i in range(16):
            tok = dmat(lts[pr], buf[pr][:, i * 512:(i + 1) * 512],
                       src[b, c * 512:(c + 1) * 512, i * 128:(i + 1) * 128])
        return tok

    def load_slab(name, s, spr):
        wt("scalar", U["slab"][spr])
        wt("scalar", cast_tok[name])
        return dma(lslab[spr], slab[spr][:].rearrange("q (i c) -> q i c",
                                                      c=512),
                   w_bf[name].rearrange("(i q) p -> q i p", q=128)
                   [:, :, s * 512:(s + 1) * 512])

    def rope(ppr, col0, t_ev):
        rt = prow[ppr]
        wt("vector", t_ev)
        wt("vector", t_cos)
        wt("vector", t_sin)
        t4 = None
        for (d0, s0) in ((0, 32), (32, 0), (64, 96), (96, 64)):
            t4 = op("vector", lambda e, d0=d0, s0=s0, rt=rt, col0=col0:
                    e.tensor_tensor(ropec[d0:d0 + 32, :], rt[s0:s0 + 32, :],
                                    sintb[s0:s0 + 32, col0:col0 + QC],
                                    ALU.mult), inc=(d0 == 96))
        t_qc = op("vector", lambda e, rt=rt, col0=col0: e.tensor_tensor(
            rt[:], rt[:], costb[:, col0:col0 + QC], ALU.mult), inc=True)
        wt("vector", t4)
        wt("vector", t_qc)
        return op("vector", lambda e, rt=rt: e.tensor_tensor(
            rt[:], rt[:], ropec[:], ALU.add), inc=True)

    def proj_tiles(bank_sel, name, s, spr, cpr, xk, sl, p_major):
        pass

    def phase_kv(b, xn1_tok):
        k_last = None
        v_last = None
        for c in range(NCH):
            cpr = c % 2
            xk = load_xnt(xnt, xn1_s, b, c, cpr, U["xnt"], pre=xn1_tok)
            for s in range(4):
                spr = s % 2
                sl = load_slab("wk", s, spr)
                for J in range(4 * s, 4 * s + 4):
                    bk = pb[J % 2]
                    bk.gw(p, "tensor")
                    wt("tensor", xk)
                    wt("tensor", sl)
                    tmm = None
                    for i in range(16):
                        tmm = op("tensor", lambda e, b_=bk, i=i, J=J, s=s,
                                 spr=spr, cpr=cpr: e.matmul(
                            b_.ap[:, :],
                            slab[spr][:, i * 512 + (J - 4 * s) * 128:
                                      i * 512 + (J - 4 * s) * 128 + 128],
                            xnt[cpr][:, i * 512:(i + 1) * 512],
                            start=(i == 0), stop=(i == 15)), inc=(i == 15))
                    bk.wr = tmm
                    U["slab"][spr] = tmm
                    U["xnt"][cpr] = tmm
                    ppr = J % 2
                    bk.gr(p, "scalar")
                    wt("scalar", U["prow"][ppr])
                    t_ev = op("scalar", lambda e, b_=bk, ppr=ppr: e.copy(
                        prow[ppr][:], b_.ap[:, :]), inc=True)
                    bk.rds.append(t_ev)
                    t_rp = rope(ppr, c * QC, t_ev)
                    wt("gpsimd", t_rp)
                    k_last = dma(lktl[ppr], kT_s[b, J * 128:(J + 1) * 128,
                                                 c * QC:(c + 1) * QC],
                                 prow[ppr][:])
                    U["prow"][ppr] = k_last
            for s in range(4):
                spr = s % 2
                sl = load_slab("wv", s, spr)
                for tt in range(4):
                    bk = pb[2 + tt % 2]
                    bk.gw(p, "tensor")
                    wt("tensor", xk)
                    wt("tensor", sl)
                    tmm = None
                    for i in range(16):
                        tmm = op("tensor", lambda e, b_=bk, i=i, tt=tt,
                                 spr=spr, cpr=cpr: e.matmul(
                            b_.ap[:, :],
                            xnt[cpr][:, i * 512 + tt * 128:
                                      i * 512 + tt * 128 + 128],
                            slab[spr][:, i * 512:(i + 1) * 512],
                            start=(i == 0), stop=(i == 15)), inc=(i == 15))
                    bk.wr = tmm
                    U["slab"][spr] = tmm
                    U["xnt"][cpr] = tmm
                    vpr = tt % 2
                    bk.gr(p, "scalar")
                    wt("scalar", U["vstg"][vpr])
                    t_ev = op("scalar", lambda e, b_=bk, vpr=vpr: e.copy(
                        vstg[vpr][:], b_.ap[:, :]), inc=True)
                    bk.rds.append(t_ev)
                    wt("gpsimd", t_ev)
                    v_last = dma(lval[vpr], va_s[b, (c * 4 + tt) * 128:
                                                 (c * 4 + tt) * 128 + 128,
                                                 s * 512:(s + 1) * 512],
                                 vstg[vpr][:])
                    U["vstg"][vpr] = v_last
        return k_last, v_last

    drain("prep", 2)
    xn1_done = {}
    xn1_done[0] = phase_ln1(0)
    drain("prep", 2)
    phase_kv(0, xn1_done[0])
    drain("late", 8)
    xn1_done[1] = phase_ln1(1)
    phase_kv(1, xn1_done[1])
    kv_mark = [(s, s.n) for s in lktl + lval]
    barrier()
    init_scope.close()

    att = ExitStack()

    def sba(name, shape, dt=BF16):
        return sb(name, shape, dt, scope=att)
    ebuf = sba("ebuf", [128, 2 * 6 * 512])
    OT = sba("OT", [128, 16 * 512])
    QTc = sba("QTc", [128, 16 * 512])
    kbuf = [sba(f"kbuf{i}", [128, 2048]) for i in range(2)]
    vbuf = [sba(f"vbuf{i}", [128, 16 * 130]) for i in range(2)]
    oub = [sba(f"oub{i}", [64, 512]) for i in range(2)]
    rbf = [sba(f"rbf{i}", [1, 512]) for i in range(2)]
    x1p = [sb(f"x1p{i}", [128, 512], F32, scope=att) for i in range(2)]
    x1r = sb("x1r", [128, D], F32, scope=att)

    def eoff(hh, slot):
        return (hh * 6 + slot) * 512

    stair = []
    for hh in range(2):
        for j in range(1, 4):
            stair.append(op("vector", lambda e, hh=hh, j=j: e.memset(
                ebuf[:, eoff(hh, j):eoff(hh, j) + 128 * j], 0.0), inc=True))

    AU = {"kbuf": [None, None], "vbuf": [None, None], "qtc": None,
          "ebuf": {}, "oub": [None, None], "rbf": [None, None],
          "x1p": [None, None], "x1r": None}

    def phase_attn_chunk(b, c, cpr, xk):
        q0 = c * QC
        qt_last = None
        for s in range(4):
            spr = s % 2
            sl = load_slab("wq", s, spr)
            for J in range(4 * s, 4 * s + 4):
                bk = pb[J % 2]
                bk.gw(p, "tensor")
                wt("tensor", xk)
                wt("tensor", sl)
                wt("tensor", AU["qtc"])
                tmm = None
                for i in range(16):
                    tmm = op("tensor", lambda e, b_=bk, i=i, J=J, s=s, spr=spr,
                             cpr=cpr: e.matmul(
                        b_.ap[:, :],
                        slab[spr][:, i * 512 + (J - 4 * s) * 128:
                                  i * 512 + (J - 4 * s) * 128 + 128],
                        xnt[cpr][:, i * 512:(i + 1) * 512],
                        start=(i == 0), stop=(i == 15)), inc=(i == 15))
                bk.wr = tmm
                U["slab"][spr] = tmm
                U["xnt"][cpr] = tmm
                ppr = J % 2
                bk.gr(p, "scalar")
                wt("scalar", U["prow"][ppr])
                t_ev = op("scalar", lambda e, b_=bk, ppr=ppr: e.copy(
                    prow[ppr][:], b_.ap[:, :]), inc=True)
                bk.rds.append(t_ev)
                t_rp = rope(ppr, c * QC, t_ev)
                wt("vector", t_rp)
                t_cp = op("vector", lambda e, J=J, ppr=ppr: e.tensor_copy(
                    QTc[:, J * 512:(J + 1) * 512], prow[ppr][:]), inc=True)
                U["prow"][ppr] = t_cp
                qt_last = t_cp

        ot_last = None
        last_s_mm = None
        for hp in range(16):
            hpr = hp % 2
            glob = (hp == 15)
            kmin = 0 if (glob or c < 2) else 512
            kend = q0 + 512
            nkt = (kend - kmin) // 128
            full_k0 = list(range(kmin, q0, 128))
            stra_k0 = [q0 + 128 * j for j in range(4)]
            wt("gpsimd", AU["kbuf"][hpr])
            wt("gpsimd", kv_mark)
            kb = dma(lkb[hpr], kbuf[hpr][:, 0:kend - kmin],
                     kT_s[b, hp * 128:(hp + 1) * 128, kmin:kend])
            wt("gpsimd", AU["vbuf"][hpr])
            for m in range(2):
                vb = dma(lvb[hpr], vbuf[hpr][:, 0:nkt * 130]
                         .rearrange("q (n h c) -> q n h c", h=2, c=65)
                         [:, :, m, 0:64],
                         va_s[b, kmin:kend,
                              hp * 128 + m * 64:hp * 128 + m * 64 + 64]
                         .rearrange("(n q) c -> q n c", q=128))
            wt("vector", vb)
            t_vo = op("vector", lambda e, hpr=hpr, nkt=nkt: e.memset(
                vbuf[hpr][:, 0:nkt * 130]
                .rearrange("q (n h c) -> q n h c", h=2, c=65)[:, :, :, 64:65],
                1.0), inc=True)

            exp_tok = {}
            for j, k0 in enumerate(stra_k0):
                for hh in range(2):
                    bk = pb[2 * (j % 2) + hh]
                    bk.gw(p, "tensor")
                    wt("tensor", kb)
                    wt("tensor", qt_last)
                    tmm = op("tensor", lambda e, b_=bk, hh=hh, k0=k0, hpr=hpr,
                             kmin=kmin, hp=hp: e.matmul(
                        b_.ap[:, :],
                        kbuf[hpr][hh * 64:hh * 64 + 64,
                                  k0 - kmin:k0 - kmin + 128],
                        QTc[hh * 64:hh * 64 + 64, hp * 512:(hp + 1) * 512],
                        start=True, stop=True, tile_position=(hh * 64, 0)),
                        inc=True)
                    bk.wr = tmm
                    last_s_mm = tmm
                    bk.gr(p, "scalar")
                    wt("scalar", AU["ebuf"].get((hh, j)))
                    t_ex = op("scalar", lambda e, b_=bk, hh=hh, j=j:
                              e.activation(
                        ebuf[:, eoff(hh, j) + 128 * j:eoff(hh, j) + 512],
                        b_.ap[:, 128 * j:512], AF.Exp, scale=SCL, bias=-SHIFT),
                        inc=True)
                    bk.rds.append(t_ex)
                    exp_tok[(hh, j)] = t_ex
            band = []
            for hh in range(2):
                for j in range(4):
                    wt("vector", exp_tok[(hh, j)])
                wt("vector", t_tri)
                for tz in stair:
                    wt("vector", tz)
                band.append(op("vector", lambda e, hh=hh: e.tensor_tensor(
                    ebuf[:, eoff(hh, 0):eoff(hh, 0) + 4 * 640]
                    .rearrange("q (j c) -> q j c", c=640)[:, :, 0:128],
                    ebuf[:, eoff(hh, 0):eoff(hh, 0) + 4 * 640]
                    .rearrange("q (j c) -> q j c", c=640)[:, :, 0:128],
                    tri4[:].rearrange("q (j c) -> q j c", c=128), ALU.mult),
                    inc=True))

            seq = [("f", k0) for k0 in full_k0] + [("s", k0) for k0 in stra_k0]
            o_tok = [None, None]
            for idx, (kind, k0) in enumerate(seq):
                first = (idx == 0)
                lastt = (idx == len(seq) - 1)
                if kind == "f":
                    slot = 4 + (idx % 2)
                    for hh in range(2):
                        bk = pb[2 * (idx % 2) + hh]
                        bk.gw(p, "tensor")
                        wt("tensor", kb)
                        wt("tensor", qt_last)
                        tmm = op("tensor", lambda e, b_=bk, hh=hh, k0=k0,
                                 hpr=hpr, kmin=kmin, hp=hp: e.matmul(
                            b_.ap[:, :],
                            kbuf[hpr][hh * 64:hh * 64 + 64,
                                      k0 - kmin:k0 - kmin + 128],
                            QTc[hh * 64:hh * 64 + 64, hp * 512:(hp + 1) * 512],
                            start=True, stop=True, tile_position=(hh * 64, 0)),
                            inc=True)
                        bk.wr = tmm
                        last_s_mm = tmm
                        bk.gr(p, "scalar")
                        wt("scalar", AU["ebuf"].get((hh, slot)))
                        t_ex = op("scalar", lambda e, b_=bk, hh=hh, slot=slot:
                                  e.activation(
                            ebuf[:, eoff(hh, slot):eoff(hh, slot) + 512],
                            b_.ap[:, :], AF.Exp, scale=SCL, bias=-SHIFT),
                            inc=True)
                        bk.rds.append(t_ex)
                        exp_tok[(hh, slot)] = t_ex
                else:
                    slot = (k0 - q0) // 128
                for hh in range(2):
                    obk = pb[4 + hh]
                    if first:
                        obk.gw(p, "tensor")
                    if kind == "f":
                        wt("tensor", exp_tok[(hh, slot)])
                    else:
                        wt("tensor", band[hh])
                    wt("tensor", vb)
                    wt("tensor", t_vo)
                    kti = (k0 - kmin) // 128
                    tmo = op("tensor", lambda e, b_=obk, hh=hh, kti=kti,
                             hpr=hpr, slot=slot, first=first, lastt=lastt:
                             e.matmul(
                        b_.ap[0:65, :],
                        vbuf[hpr][:, kti * 130 + hh * 65:
                                  kti * 130 + hh * 65 + 65],
                        ebuf[:, eoff(hh, slot):eoff(hh, slot) + 512],
                        start=first, stop=lastt), inc=True)
                    o_tok[hh] = tmo
                    if kind == "f":
                        AU["ebuf"][(hh, slot)] = tmo
            pb[4].wr = o_tok[0]
            pb[5].wr = o_tok[1]
            AU["kbuf"][hpr] = o_tok[1]
            AU["vbuf"][hpr] = o_tok[1]
            for j in range(4):
                AU["ebuf"][(0, j)] = o_tok[0]
                AU["ebuf"][(1, j)] = o_tok[1]

            for hh in range(2):
                bk = pb[4 + hh]
                bk.gr(p, "vector")
                wt("vector", AU["rbf"][hh])
                def _rcp(e, b_=bk, hh=hh):
                    with nc.allow_low_precision(
                            reason="1/n row in bf16 feeds a bf16 broadcast "
                                   "matmul; 2^-9 uniform scale error"):
                        return e.reciprocal(rbf[hh][:], b_.ap[64:65, :])
                t_rc = op("vector", _rcp, inc=True)
                bk.gr(p, "scalar")
                wt("scalar", AU["oub"][hh])
                t_ou = op("scalar", lambda e, b_=bk, hh=hh: e.copy(
                    oub[hh][:], b_.ap[0:64, :]), inc=True)
                bk.rds.append(t_rc)
                bk.rds.append(t_ou)
                rbk = pb[6 + hh]
                rbk.gw(p, "tensor")
                wt("tensor", t_rc)
                wt("tensor", t_ones)
                t_rb = op("tensor", lambda e, b_=rbk, hh=hh: e.matmul(
                    b_.ap[0:64, :], onesb[0:1, 0:64], rbf[hh][:],
                    start=True, stop=True), inc=True)
                rbk.wr = t_rb
                AU["rbf"][hh] = t_rb
                rbk.gr(p, "vector")
                wt("vector", t_ou)
                t_ot = op("vector", lambda e, b_=rbk, hh=hh, hp=hp:
                          e.tensor_tensor(
                    OT[hh * 64:hh * 64 + 64, hp * 512:(hp + 1) * 512],
                    oub[hh][:], b_.ap[0:64, :], ALU.mult), inc=True)
                rbk.rds.append(t_ot)
                AU["oub"][hh] = t_ot
                ot_last = t_ot
        AU["qtc"] = last_s_mm
        return ot_last

    def phase_wo(b, c, ot_last):
        x1_tok = {}
        for s in range(4):
            spr = s % 2
            sl = load_slab("wo", s, spr)
            for tau in range(4):
                trow = c * 512 + tau * 128
                bk = pb[tau]
                bk.gw(p, "tensor")
                wt("tensor", sl)
                wt("tensor", ot_last)
                tmm = None
                for i in range(16):
                    tmm = op("tensor", lambda e, b_=bk, i=i, tau=tau, spr=spr:
                             e.matmul(
                        b_.ap[:, :],
                        OT[:, i * 512 + tau * 128:i * 512 + tau * 128 + 128],
                        slab[spr][:, i * 512:(i + 1) * 512],
                        start=(i == 0), stop=(i == 15)), inc=(i == 15))
                bk.wr = tmm
                U["slab"][spr] = tmm
                xpr = tau % 2
                wt("scalar", U["xb"][xpr])
                tkx = dma(lxb[xpr], xb[xpr][:, 0:512],
                          x_in[b, trow:trow + 128, s * 512:(s + 1) * 512])
                bk.gr(p, "vector")
                wt("vector", tkx)
                wt("vector", t_BO)
                wt("vector", AU["x1p"][xpr])
                t_a1 = op("vector", lambda e, b_=bk, s=s, xpr=xpr:
                          e.tensor_tensor(
                    x1p[xpr][:], b_.ap[:, :], BO[:, s * 512:(s + 1) * 512],
                    ALU.add), inc=True)
                wt("vector", t_a1)
                t_a2 = op("vector", lambda e, xpr=xpr: e.tensor_tensor(
                    x1p[xpr][:], x1p[xpr][:], xb[xpr][:, 0:512], ALU.add),
                    inc=True)
                bk.rds.append(t_a2)
                U["xb"][xpr] = t_a2
                wt("gpsimd", t_a2)
                stx = dma(lx1s[xpr],
                          x1_s[b, trow:trow + 128, s * 512:(s + 1) * 512],
                          x1p[xpr][:])
                AU["x1p"][xpr] = stx
                x1_tok[(s, tau)] = stx
        for tau in range(4):
            trow = c * 512 + tau * 128
            for s in range(4):
                wt("scalar", x1_tok[(s, tau)])
            wt("scalar", AU["x1r"])
            tk = dma(lx1r, x1r[:], x1_s[b, trow:trow + 128, :])
            pr = tau % 2
            wt("vector", U["xnst"][pr])
            t_xn = layer_norm(x1r[:], xnst[pr][:], pr, tk)
            AU["x1r"] = t_xn
            wt("gpsimd", t_xn)
            last_xn2 = dma(lsty[pr], xn2_s[b, trow:trow + 128, :], xnst[pr][:])
            U["xnst"][pr] = last_xn2

    for b in range(B):
        for c in range(NCH):
            cpr = c % 2
            xk = load_xnt(xnt, xn1_s, b, c, cpr, U["xnt"], pre=xn1_done[b])
            ot_last = phase_attn_chunk(b, c, cpr, xk)
            phase_wo(b, c, ot_last)
            drain("late", 16)
    drain("late", len(prep_late))
    x1_mark = [(s, s.n) for s in lx1s]
    xn2_mark = [(s, s.n) for s in lsty]
    fold_mark = [(s, s.n) for s in lfs]
    barrier()
    att.close()
    era.close()

    ffn = ExitStack()

    def sbf(name, shape, dt=BF16):
        return sb(name, shape, dt, scope=ffn)
    xnt2 = xnt
    zbuf = sbf("zbuf", [128, 64 * 512])
    fslab = [sbf(f"fslab{i}", [128, 32 * 512]) for i in range(2)]
    gbuf = [sbf(f"gbuf{i}", [128, 512]) for i in range(2)]
    w2b = [sbf(f"w2b{i}", [128, 8 * 512]) for i in range(2)]
    x1b = [sb(f"x1b{i}", [128, 512], F32, scope=ffn) for i in range(2)]
    x2b = [sb(f"x2b{i}", [128, 512], F32, scope=ffn) for i in range(2)]

    FU = {"xnt2": [None, None], "fslab": [None, None], "gbuf": [None, None],
          "w2b": [None, None], "x1b": [None, None], "x2b": [None, None],
          "zbuf": None, "w2tok": [None, None]}
    out_last = None
    for b in range(B):
        for c in range(NCH):
            cpr = c % 2
            xk = load_xnt(xnt2, xn2_s, b, c, cpr, FU["xnt2"], pre=xn2_mark)
            z_writes = []
            for s in range(16):
                spr = s % 2
                wt("scalar", FU["fslab"][spr])
                wt("scalar", cast_tok["w1"])
                wt("scalar", cast_tok["w3"])
                t1 = dma(lslab[spr], fslab[spr][:, 0:16 * 512]
                         .rearrange("q (i c) -> q i c", c=512),
                         w_bf["w1"].rearrange("(i q) h -> q i h", q=128)
                         [:, :, s * 512:(s + 1) * 512])
                t3 = dma(lslab[spr], fslab[spr][:, 16 * 512:32 * 512]
                         .rearrange("q (i c) -> q i c", c=512),
                         w_bf["w3"].rearrange("(i q) h -> q i h", q=128)
                         [:, :, s * 512:(s + 1) * 512])
                for J in range(4 * s, 4 * s + 4):
                    jpr = J % 2
                    bka = pb[0 + 2 * jpr]
                    bkb = pb[1 + 2 * jpr]
                    for bk, woff, tw in ((bka, 0, t1), (bkb, 16 * 512, t3)):
                        bk.gw(p, "tensor")
                        wt("tensor", xk)
                        wt("tensor", tw)
                        tmm = None
                        for i in range(16):
                            tmm = op("tensor", lambda e, b_=bk, i=i, J=J, s=s,
                                     spr=spr, woff=woff, cpr=cpr: e.matmul(
                                b_.ap[:, :],
                                fslab[spr][:, woff + i * 512 +
                                           (J - 4 * s) * 128:
                                           woff + i * 512 +
                                           (J - 4 * s) * 128 + 128],
                                xnt2[cpr][:, i * 512:(i + 1) * 512],
                                start=(i == 0), stop=(i == 15)),
                                inc=(i == 15))
                        bk.wr = tmm
                        FU["fslab"][spr] = tmm
                        FU["xnt2"][cpr] = tmm
                    gpr = J % 2
                    bka.gr(p, "scalar")
                    wt("scalar", FU["gbuf"][gpr])
                    t_si = op("scalar", lambda e, b_=bka, gpr=gpr:
                              e.activation(gbuf[gpr][:], b_.ap[:, :], AF.Silu),
                              inc=True)
                    bka.rds.append(t_si)
                    bkb.gr(p, "vector")
                    wt("vector", t_si)
                    t_g = op("vector", lambda e, b_=bkb, gpr=gpr:
                             e.scalar_tensor_tensor(
                        gbuf[gpr][:], b_.ap[:, :], GSC, gbuf[gpr][:],
                        ALU.mult, ALU.mult), inc=True)
                    bkb.rds.append(t_g)
                    zbk = pb[4 + jpr]
                    zbk.gw(p, "tensor")
                    wt("tensor", t_g)
                    wt("tensor", t_hm)
                    t_z = op("tensor", lambda e, b_=zbk, gpr=gpr: e.matmul(
                        b_.ap[:, :], hmat[:], gbuf[gpr][:], start=True,
                        stop=True), inc=True)
                    zbk.wr = t_z
                    FU["gbuf"][gpr] = t_z
                    zbk.gr(p, "scalar")
                    if FU["zbuf"] is not None:
                        wt("scalar", FU["zbuf"])
                    t_ze = op("scalar", lambda e, b_=zbk, J=J: e.copy(
                        zbuf[:, J * 512:(J + 1) * 512], b_.ap[:, :]),
                        inc=True)
                    zbk.rds.append(t_ze)
                    z_writes.append(t_ze)

            last_mm = None
            for ds in range(4):
                banks = [pb[k] for k in range(4)]
                for tau in range(4):
                    banks[tau].gw(p, "tensor")
                for g in range(2):
                    wt("scalar", FU["w2b"][g])
                    wt("scalar", fold_mark)
                    FU["w2tok"][g] = dma(
                        lw2b[g], w2b[g][:].rearrange("q (j c) -> q j c", c=512),
                        w2fold.rearrange("(j q) d -> q j d", q=128)
                        [:, g * 8:(g + 1) * 8, ds * 512:(ds + 1) * 512])
                for J in range(64):
                    g = J // 8
                    gpr2 = g % 2
                    wt("tensor", FU["w2tok"][gpr2])
                    wt("tensor", z_writes[J])
                    grp_last = None
                    for tau in range(4):
                        bk = banks[tau]
                        tmm = op("tensor", lambda e, b_=bk, J=J, tau=tau,
                                 gpr2=gpr2: e.matmul(
                            b_.ap[:, :],
                            zbuf[:, J * 512 + tau * 128:
                                 J * 512 + tau * 128 + 128],
                            w2b[gpr2][:, (J % 8) * 512:(J % 8 + 1) * 512],
                            start=(J == 0), stop=(J == 63)),
                            inc=(J == 63 or tau == 3))
                        grp_last = tmm
                        if J == 63:
                            bk.wr = tmm
                            last_mm = tmm
                    if J % 8 == 7 and J < 48:
                        FU["w2b"][gpr2] = grp_last
                        wt("scalar", grp_last)
                        FU["w2tok"][gpr2] = dma(
                            lw2b[gpr2], w2b[gpr2][:]
                            .rearrange("q (j c) -> q j c", c=512),
                            w2fold.rearrange("(j q) d -> q j d", q=128)
                            [:, (g + 2) * 8:(g + 3) * 8,
                             ds * 512:(ds + 1) * 512])
                for tau in range(4):
                    bk = banks[tau]
                    trow = c * 512 + tau * 128
                    xpr = tau % 2
                    wt("scalar", FU["x1b"][xpr])
                    wt("scalar", x1_mark)
                    tk1 = dma(lxb[xpr], x1b[xpr][:],
                              x1_s[b, trow:trow + 128,
                                   ds * 512:(ds + 1) * 512])
                    bk.gr(p, "vector")
                    wt("vector", tk1)
                    wt("vector", FU["x2b"][xpr])
                    t_x2 = op("vector", lambda e, b_=bk, xpr=xpr:
                              e.tensor_tensor(
                        x2b[xpr][:], b_.ap[:, :], x1b[xpr][:], ALU.add),
                        inc=True)
                    bk.rds.append(t_x2)
                    FU["x1b"][xpr] = t_x2
                    wt("gpsimd", t_x2)
                    out_last = dma(lout[xpr], out_t[b, trow:trow + 128,
                                              ds * 512:(ds + 1) * 512],
                                   x2b[xpr][:])
                    FU["x2b"][xpr] = out_last
            FU["zbuf"] = last_mm

    if _DBG:
        for b in range(B):
            wt("sync", x1_mark)
            dma(ldbg, x1d_t[b], x1_s[b])
            wt("gpsimd", kv_mark)
            dma(lcdbg, ktd_t[b], kT_s[b])
            dma(lcdbg, vad_t[b], va_s[b])
    for _s in sems.values():
        p.wait("sync", (_s, _s.n))
    ffn.close()

    with nc.Block() as blk:
        p.run(blk)
    es.close()
    return nc


def _tables():
    inv = 10000.0 ** (-(np.arange(0, HD, 2, dtype=np.float64) / HD))
    pos = np.arange(T, dtype=np.float64)
    fr = pos[None, :] * inv[:, None]
    cos32 = np.cos(fr).astype(np.float32)
    sin32 = np.sin(fr).astype(np.float32)
    cost = np.zeros((128, T), dtype=np.float32)
    sint = np.zeros((128, T), dtype=np.float32)
    for half in range(2):
        b0 = half * 64
        cost[b0:b0 + 32] = cos32
        cost[b0 + 32:b0 + 64] = cos32
        sint[b0:b0 + 32] = sin32
        sint[b0 + 32:b0 + 64] = -sin32
    qq = np.arange(128)[None, :]
    kk = np.arange(128)[:, None]
    tri = (qq >= kk).astype(np.float32).astype(BFD)
    pc = np.zeros((128, 128), dtype=np.int64)
    for i in range(128):
        for j in range(128):
            pc[i, j] = bin(i & j).count("1")
    hmat = np.where(pc % 2 == 0, 1.0, -1.0).astype(np.float32).astype(BFD)
    return cost, sint, tri, hmat


_NC_CACHE = None


def kernel(L0_attn_hidden_states, L0_attn_Wq, L0_attn_Wk, L0_attn_Wv, L0_attn_Wo,
           L0_attn_bo, L0_ffn_W1, L0_ffn_W2, L0_ffn_W3):
    global _NC_CACHE
    x = np.ascontiguousarray(L0_attn_hidden_states, dtype=np.float32)
    cost, sint, tri, hmat = _tables()
    if _NC_CACHE is None:
        _NC_CACHE = build_nc()
    nc = _NC_CACHE
    base = {
        "wq": np.ascontiguousarray(np.asarray(L0_attn_Wq, dtype=np.float32).T),
        "wk": np.ascontiguousarray(np.asarray(L0_attn_Wk, dtype=np.float32).T),
        "wv": np.ascontiguousarray(np.asarray(L0_attn_Wv, dtype=np.float32).T),
        "wo": np.ascontiguousarray(np.asarray(L0_attn_Wo, dtype=np.float32).T),
        "bo": np.ascontiguousarray(L0_attn_bo, dtype=np.float32).reshape(1, D),
        "w1": np.ascontiguousarray(np.asarray(L0_ffn_W1, dtype=np.float32).T),
        "w2": np.ascontiguousarray(np.asarray(L0_ffn_W2, dtype=np.float32).T),
        "w3": np.ascontiguousarray(np.asarray(L0_ffn_W3, dtype=np.float32).T),
        "cost_t": cost, "sint_t": sint,
        "tri_t": np.asarray(tri), "hmat_t": np.asarray(hmat),
    }
    in_maps = []
    for core in range(8):
        m = dict(base)
        m["x"] = x[core * 2:core * 2 + 2]
        in_maps.append(m)
    res = run_bass_kernel_spmd(nc, in_maps, core_ids=list(range(8)))
    global _DBG_X1, _DBG_KT, _DBG_VA
    if "x1d" in res.results[0]:
        _DBG_X1 = np.concatenate([r["x1d"] for r in res.results], axis=0)
        _DBG_KT = np.concatenate([r["ktd"] for r in res.results], axis=0)
        _DBG_VA = np.concatenate([r["vad"] for r in res.results], axis=0)
    return np.concatenate([r["out"] for r in res.results], axis=0)
